# version 09
# baseline (speedup 1.0000x reference)
import contextlib
from collections import defaultdict

import numpy as np
import ml_dtypes
import concourse.bass as bass
import concourse.mybir as mybir
from concourse.bass_utils import run_bass_kernel_spmd

F32 = mybir.dt.float32
F16 = mybir.dt.float16
BF16 = mybir.dt.bfloat16
AF = mybir.ActivationFunctionType
ALU = mybir.AluOpType
AX = mybir.AxisListType

B, S, D, H = 16, 2048, 1024, 4096
NCORES = 8
EPB = B // NCORES
NT = S // 128
NDC = D // 128
NCC = H // 128
SCH = 512
NSCH = S // SCH
MAGIC = float(np.float32(1.5 * 2 ** 23))
QTW = 512
NQT = S // QTW

ENGINES = ("sync", "tensor", "vector", "scalar", "gpsimd")
SKIP_PHASES = set()
SPACERS = True


class P:

    def __init__(self):
        self.q = {e: [] for e in ENGINES}
        self.log = {e: [] for e in ENGINES}
        self.cnt = defaultdict(int)
        self.hi = defaultdict(int)
        self.block = None

    def flush(self):
        for e in ENGINES:
            items = self.q[e]
            if not items:
                continue
            self.q[e] = []
            self.log[e].extend(items)

            def body(eng, items=items, e=e):
                for kind, a, b, c in items:
                    if kind == "wait":
                        eng.wait_ge(self.sems[a], b)
                    elif a is None:
                        f = self.fence
                        if e == "vector":
                            eng.tensor_copy(f[0:1, 0:1], f[0:1, 1:2]).then_inc(self.sems[b], c)
                        elif e == "scalar":
                            eng.activation(f[0:1, 2:3], f[0:1, 3:4],
                                           mybir.ActivationFunctionType.Copy).then_inc(self.sems[b], c)
                        elif e == "gpsimd":
                            eng.memset(f[0:1, 4:5], 0.0).then_inc(self.sems[b], c)
                        else:
                            eng.nop().then_inc(self.sems[b], c)
                    elif b is None:
                        a(eng)
                    else:
                        a(eng).then_inc(self.sems[b], c)
            getattr(self.block, e)(body)

    def op(self, eng, fn, sem=None, amt=1):
        if sem is None:
            self.q[eng].append(("op", fn, None, 0))
            return None
        self.cnt[sem] += amt
        self.q[eng].append(("op", fn, sem, amt))
        return self.cnt[sem]

    def wt(self, eng, sem, val):
        if not val or val <= 0:
            return
        if self.hi[(eng, sem)] >= val:
            return
        self.hi[(eng, sem)] = val
        self.q[eng].append(("wait", sem, val, 0))

    def bar(self, parts=("tensor", "vector", "scalar", "gpsimd")):
        for e in parts:
            self.cnt["bar"] += 1
            self.q[e].append(("op", None, "bar", 1))
        tgt = self.cnt["bar"]
        for e in parts:
            self.wt(e, "bar", tgt)

    def check_deadlock(self):
        full = {e: self.log[e] + self.q[e] for e in ENGINES}
        pos = {e: 0 for e in ENGINES}
        cur = defaultdict(int)
        progress = True
        while progress:
            progress = False
            for e in ENGINES:
                q = full[e]
                while pos[e] < len(q):
                    kind, a, b, c = q[pos[e]]
                    if kind == "wait":
                        if cur[a] < b:
                            break
                    else:
                        if b is not None:
                            cur[b] += c
                    pos[e] += 1
                    progress = True
        stuck = {e: (pos[e], len(full[e])) for e in ENGINES if pos[e] < len(full[e])}
        if stuck:
            for e, (i, n) in stuck.items():
                kind, a, b, c = full[e][i]
                print(f"DEADLOCK {e} at {i}/{n}: waiting {a} >= {b}, have {cur[a]}")
            raise RuntimeError(f"deadlock: {stuck}")

    def replay(self, block, sems):
        def mk(name):
            def body(eng):
                for kind, a, b, c in self.q[name]:
                    if kind == "wait":
                        eng.wait_ge(sems[a], b)
                    elif a is None:
                        eng.nop().then_inc(sems[b], c)
                    elif b is None:
                        a(eng)
                    else:
                        a(eng).then_inc(sems[b], c)
            return body
        for e in ENGINES:
            getattr(block, e)(mk(e))


def build_program(dbg=None):
    nc = bass.Bass("TRN2", target_bir_lowering=False, detect_race_conditions=False)
    dbg_outs = {}

    x_in = nc.dram_tensor("x", [EPB, S, D], F32, kind="ExternalInput").ap()
    wqT_h = nc.dram_tensor("wqT", [D, D], F32, kind="ExternalInput").ap()
    wkT_h = nc.dram_tensor("wkT", [D, D], F32, kind="ExternalInput").ap()
    wvT_h = nc.dram_tensor("wvT", [D, D], F32, kind="ExternalInput").ap()
    bq_h = nc.dram_tensor("bq", [128, 8], F32, kind="ExternalInput").ap()
    bk_h = nc.dram_tensor("bk", [128, 8], F32, kind="ExternalInput").ap()
    bv_h = nc.dram_tensor("bv", [1, D], F32, kind="ExternalInput").ap()
    w1T_h = nc.dram_tensor("w1T", [D, H], F32, kind="ExternalInput").ap()
    w2T_h = nc.dram_tensor("w2T", [H, D], F32, kind="ExternalInput").ap()
    cw_h = nc.dram_tensor("cw", [128, 96], F32, kind="ExternalInput").ap()
    cb_h = nc.dram_tensor("cb", [128, 32], F32, kind="ExternalInput").ap()
    cos_h = nc.dram_tensor("cost", [4, 128, S], F16, kind="ExternalInput").ap()
    sin_h = nc.dram_tensor("sint", [4, 128, S], F16, kind="ExternalInput").ap()
    id16_h = nc.dram_tensor("id16", [128, 128], F16, kind="ExternalInput").ap()
    ones_h = nc.dram_tensor("onesr", [1, 128], F32, kind="ExternalInput").ap()
    out_h = nc.dram_tensor("out", [EPB, S, D], F32, kind="ExternalOutput").ap()
    w1q_h = nc.dram_tensor("w1q_scr", [NCC, 128, NDC * 128], F16, kind="Internal").ap()
    w2q_h = nc.dram_tensor("w2q_scr", [NCC, 128, D], F16, kind="Internal").ap()

    p = P()
    root = contextlib.ExitStack()
    with root:
        def sb(stack, name, shape, dt, side="left"):
            return stack.enter_context(nc.sbuf_tensor(name, shape, dt, side=side))

        def psum(stack, name, shape, dt):
            return stack.enter_context(nc.psum_tensor(name, shape, dt))

        id16 = sb(root, "sb_id16", [128, 128], F16)
        onesr = sb(root, "sb_onesr", [1, 128], F32)
        onesr16 = sb(root, "onesr16", [1, 128], F16)
        onescF = sb(root, "onescF", [128, 1], F32)
        onesc_bf = sb(root, "onesc_bf", [128, 1], BF16)
        cneg = sb(root, "cneg", [128, 1], F32)
        zcol = sb(root, "zcol", [128, 1], F32)
        bq_t = sb(root, "bq_t", [128, 8], F32)
        bk_t = sb(root, "bk_t", [128, 8], F32)
        bv_t = sb(root, "bv_t", [1, D], F32)
        cw_t = sb(root, "cw_t", [128, 96], F32)
        cb_t = sb(root, "cb_t", [128, 32], F32)
        mw1c = sb(root, "mw1c", [128, 1], F32)
        sw1c = sb(root, "sw1c", [128, 1], F32)
        mw2c = sb(root, "mw2c", [128, 1], F32)
        sw2c = sb(root, "sw2c", [128, 1], F32)
        fence_t = sb(root, "fence_t", [128, 8], F32)
        dbgT = sb(root, "dbgT", [128, 64], F32)
        xbuf = [sb(root, f"xbuf{i}", [128, D], F32) for i in range(2)]
        scr1k = [sb(root, f"scr1k{i}", [128, D], F32) for i in range(2)]
        sa = sb(root, "statA", [128, 128], F32)
        srow = sb(root, "sb_srow", [1, 512], F16)
        hhalo = sb(root, "hhalo", [128, NCC * 6], F16)
        xqhalo = sb(root, "xqhalo", [128, NDC * 6], F16)
        bh16 = sb(root, "bh16", [128, 512], F16)
        mmax = sb(root, "mmax", [128, 512], F16)
        mmin = sb(root, "mmin", [128, 512], F16)

        C_SUM, C_SSQ, C_M, C_VAR, C_R, C_NB, C_T1, C_T2 = 0, 16, 32, 48, 64, 80, 96, 112

        semnames = ["bar", "dS", "dG", "dO", "T", "V", "SC", "G"]
        sems = {n: root.enter_context(nc.semaphore(n)) for n in semnames}
        with nc.Block() as blk0:
            @blk0.gpsimd
            def _(g):
                for h in sems.values():
                    g.sem_clear(h)
        p.sems = sems
        p.fence = fence_t
        p.block = root.enter_context(nc.Block())

        def dma(eng, dst, src):
            sem = {"sync": "dS", "gpsimd": "dG", "scalar": "dO"}[eng]
            return p.op(eng, lambda e, d=dst, s=src: e.dma_start(out=d, in_=s), sem, 16)

        xbuf_g = {0: [], 1: []}

        def xbuf_acquire(slot):
            for sem, val in xbuf_g[slot]:
                p.wt("sync", sem, val)
            xbuf_g[slot] = []

        def spacer(eng="vector"):
            if SPACERS:
                p.op(eng, lambda e: e.tensor_copy(fence_t[:, 5:6], fence_t[:, 6:7]))
                p.op(eng, lambda e: e.tensor_copy(fence_t[:, 6:7], fence_t[:, 7:8]))

        def dbg_dump(name, src_ap, shape, dt, prod_sem, prod_val):
            if dbg is None:
                return
            t = nc.dram_tensor("dbg_" + name, shape, dt, kind="ExternalOutput").ap()
            dbg_outs[name] = t
            p.wt("sync", prod_sem, prod_val)
            dma("sync", t, src_ap)

        for dst, src in [(id16[:], id16_h), (onesr[:], ones_h), (bq_t[:], bq_h),
                         (bk_t[:], bk_h), (bv_t[:], bv_h), (cw_t[:], cw_h),
                         (cb_t[:], cb_h)]:
            dma("sync", dst, src)
        init_dS = p.cnt["dS"]
        p.op("gpsimd", lambda e: e.memset(cneg[:], -16.0))
        p.op("gpsimd", lambda e: e.memset(zcol[:], 0.0))
        p.op("gpsimd", lambda e: e.memset(onescF[:], 1.0))
        p.op("gpsimd", lambda e: e.memset(onesc_bf[:], 1.0))
        g_init = p.op("gpsimd", lambda e: e.memset(onesr16[:], 1.0), "G")

        prep_scr_dS = p.cnt["dS"]
        with contextlib.ExitStack() as ps_prep:
          if "PREP" not in SKIP_PHASES:
              wbuf = [sb(ps_prep, f"wbuf{i}", [128, H], F32, side="right") for i in range(2)]
              wq16 = sb(ps_prep, "wq16", [128, H], F16, side="right")
              pprep = psum(ps_prep, "p_prep", [128, 512], F32)
              p.wt("tensor", "dS", init_dS)
              p.wt("vector", "dS", init_dS)

              def prep_weight(wT_hbm, n_tiles, ncols, mwc, swc, store):
                  ntot = float(n_tiles * 128 * ncols)
                  p.wt("tensor", "V", p.cnt["V"])
                  vred = {}
                  for i in range(n_tiles):
                      bufi = wbuf[i % 2]
                      if i >= 2:
                          p.wt("sync", "V", vred.get(i - 2))
                      v = dma("sync", bufi[:, 0:ncols], wT_hbm[i * 128:(i + 1) * 128, :])
                      p.wt("vector", "dS", v)
                      vred[i] = p.op("vector", lambda e, b=bufi, i=i: e.reduce_sum(
                          sa[:, 64 + i:65 + i], b[:, 0:ncols], axis=AX.X,
                          apply_absolute_value=True), "V")
                  spacer()
                  p.op("vector", lambda e, n=n_tiles: e.reduce_sum(
                      sa[:, 62:63], sa[:, 64:64 + n], axis=AX.X))
                  spacer()
                  vv = p.op("vector", lambda e: e.tensor_copy(fence_t[:, 5:6], fence_t[:, 6:7]), "V")
                  p.wt("tensor", "V", vv)
                  tv = p.op("tensor", lambda e: e.matmul(
                      pprep[0:1, 0:1], sa[:, 62:63], onescF[:], start=True, stop=True), "T")
                  p.wt("vector", "T", tv)
                  p.op("vector", lambda e: e.tensor_scalar(
                      sa[0:1, 60:61], pprep[0:1, 0:1], 1.0 / ntot, 1e-5,
                      ALU.mult, ALU.max))
                  spacer()
                  vv = p.op("vector", lambda e: e.tensor_copy(fence_t[:, 5:6], fence_t[:, 6:7]), "V")
                  p.wt("tensor", "V", vv)
                  tv = p.op("tensor", lambda e: e.matmul(
                      pprep[:, 4:5], onesr[:], sa[0:1, 60:61], start=True, stop=True), "T")
                  p.wt("vector", "T", tv)
                  p.op("vector", lambda e: e.tensor_scalar(
                      mwc[:], pprep[:, 4:5], 1.0 / 127.0, None, ALU.mult))
                  p.op("vector", lambda e: e.reciprocal(swc[:], pprep[:, 4:5]))
                  spacer()
                  vv = p.op("vector", lambda e: e.tensor_copy(fence_t[:, 5:6], fence_t[:, 6:7]), "V")
                  vq_hist = {}
                  store_dS = {}
                  for i in range(n_tiles):
                      bufi = wbuf[i % 2]
                      if i >= 2:
                          p.wt("sync", "V", vq_hist.get(i - 2))
                      v = dma("sync", bufi[:, 0:ncols], wT_hbm[i * 128:(i + 1) * 128, :])
                      p.wt("vector", "dS", v)
                      p.op("vector", lambda e, b=bufi, n=ncols: e.tensor_scalar(
                          b[:, 0:n], b[:, 0:n], swc[:, 0:1], MAGIC, ALU.mult, ALU.add))
                      p.op("vector", lambda e, b=bufi, n=ncols: e.tensor_scalar(
                          b[:, 0:n], b[:, 0:n], MAGIC, 1.0, ALU.subtract, ALU.min))
                      if i >= 1:
                          p.wt("vector", "dS", store_dS.get(i - 1))
                      p.op("vector", lambda e, b=bufi, n=ncols: e.tensor_scalar(
                          wq16[:, 0:n], b[:, 0:n], -1.0, None, ALU.max))
                      spacer()
                      vq = p.op("vector", lambda e: e.tensor_copy(fence_t[:, 5:6], fence_t[:, 6:7]), "V")
                      vq_hist[i] = vq
                      p.wt("sync", "V", vq)
                      store(i)
                      store_dS[i] = p.cnt["dS"]

              def w1_store(i):
                  for j in range(NCC):
                      dma("sync", w1q_h[j, :, i * 128:(i + 1) * 128],
                          wq16[:, j * 128:(j + 1) * 128])

              def w2_store(i):
                  dma("sync", w2q_h[i], wq16[:, 0:D])

              prep_weight(w1T_h, NDC, H, mw1c, sw1c, w1_store)
              prep_weight(w2T_h, NCC, D, mw2c, sw2c, w2_store)
              prep_scr_dS = p.cnt["dS"]
              p.bar()
              p.flush()

        for e_ in ("vector", "scalar", "tensor", "gpsimd"):
            p.wt(e_, "dS", prep_scr_dS)
        for ei in range(EPB):
            with contextlib.ExitStack() as es_e:
                es_ab = es_e.enter_context(contextlib.ExitStack())
                q16 = sb(es_ab, f"q16_{ei}", [128, NDC * 2048], F16, side="left")
                k16 = sb(es_ab, f"k16_{ei}", [128, NDC * 2048], F16, side="left")
                v_bf = sb(es_ab, f"vbf_{ei}", [128, NT * 1024], BF16, side="left")

                with contextlib.ExitStack() as es_a:
                    xn_fm = sb(es_a, f"xnfm_{ei}", [128, NDC * 2048], F16, side="right")
                    wT16 = sb(es_a, f"wT16_{ei}", [128, NDC * 1024], F16, side="right")
                    costb = [sb(es_a, f"cosb{i}_{ei}", [128, S], F16, side="right") for i in range(2)]
                    sintb = [sb(es_a, f"sinb{i}_{ei}", [128, S], F16, side="right") for i in range(2)]
                    rsc = [sb(es_a, f"rsc{i}_{ei}", [128, S], F16, side="right") for i in range(3)]
                    xn_tm = [sb(es_a, f"xntm{i}_{ei}", [128, D], F16, side="right") for i in range(2)]

                    with contextlib.ExitStack() as es_p:
                        ps_tr = [psum(es_p, f"pstr{i}_{ei}", [128, 1024], F16) for i in range(2)]
                        v_red = {}
                        s_sq = {}
                        ssq_red = {}
                        for t in range(NT):
                            xbuf_acquire(t % 2)
                            v = dma("sync", xbuf[t % 2][:], x_in[ei, t * 128:(t + 1) * 128, :])
                            p.wt("vector", "dS", v)
                            v_red[t] = p.op("vector", lambda e, t=t, b=xbuf[t % 2]: e.reduce_sum(
                                sa[:, C_SUM + t:C_SUM + t + 1], b[:], axis=AX.X), "V")
                            p.wt("scalar", "dS", v)
                            if t >= 2:
                                p.wt("scalar", "V", ssq_red.get(t - 2))
                            s_sq[t] = p.op("scalar", lambda e, t=t, b=xbuf[t % 2]: e.activation(
                                scr1k[t % 2][:], b[:], AF.Square), "SC")
                            p.wt("vector", "SC", s_sq[t])
                            ssq_red[t] = p.op("vector", lambda e, t=t: e.reduce_sum(
                                sa[:, C_SSQ + t:C_SSQ + t + 1], scr1k[t % 2][:], axis=AX.X), "V")
                            xbuf_g[t % 2] = [("V", v_red[t]), ("SC", s_sq[t])]

                        def stats_batch():
                            p.op("vector", lambda e: e.tensor_scalar(
                                sa[:, C_M:C_M + 16], sa[:, C_SUM:C_SUM + 16], 1.0 / D, None, ALU.mult))
                            spacer()
                            p.op("vector", lambda e: e.tensor_scalar(
                                sa[:, C_T1:C_T1 + 16], sa[:, C_SSQ:C_SSQ + 16], 1.0 / D, None, ALU.mult))
                            spacer()

                            p.op("vector", lambda e: e.tensor_tensor(
                                sa[:, C_T2:C_T2 + 16], sa[:, C_M:C_M + 16], sa[:, C_M:C_M + 16], ALU.mult))
                            spacer()

                            p.op("vector", lambda e: e.tensor_tensor(
                                sa[:, C_VAR:C_VAR + 16], sa[:, C_T1:C_T1 + 16], sa[:, C_T2:C_T2 + 16], ALU.subtract))
                            spacer()
                            vv = p.op("vector", lambda e: e.tensor_scalar(
                                sa[:, C_VAR:C_VAR + 16], sa[:, C_VAR:C_VAR + 16], 1e-5, None, ALU.add), "V")
                            if dbg is not None and ei == 0:
                                p.op("vector", lambda e: e.tensor_copy(dbgT[:, 16:32], sa[:, C_VAR:C_VAR + 16]))
                            p.wt("scalar", "V", vv)
                            if dbg is not None and ei == 0:
                                p.op("scalar", lambda e: e.activation(dbgT[:, 32:48], sa[:, C_VAR:C_VAR + 16], AF.Copy))
                            sv = p.op("scalar", lambda e: e.activation(
                                sa[:, C_T1:C_T1 + 16], sa[:, C_VAR:C_VAR + 16], AF.Ln, bias=zcol[:, 0:1]), "SC")
                            sv = p.op("scalar", lambda e: e.activation(
                                sa[:, C_R:C_R + 16], sa[:, C_T1:C_T1 + 16], AF.Exp,
                                bias=zcol[:, 0:1], scale=-0.5), "SC")
                            p.wt("vector", "SC", sv)
                            p.op("vector", lambda e: e.tensor_tensor(
                                sa[:, C_T1:C_T1 + 16], sa[:, C_R:C_R + 16], sa[:, C_R:C_R + 16], ALU.mult))
                            spacer()
                            p.op("vector", lambda e: e.tensor_tensor(
                                sa[:, C_T2:C_T2 + 16], sa[:, C_VAR:C_VAR + 16], sa[:, C_T1:C_T1 + 16], ALU.mult))
                            spacer()
                            p.op("vector", lambda e: e.tensor_scalar(
                                sa[:, C_T2:C_T2 + 16], sa[:, C_T2:C_T2 + 16], -0.5, 1.5, ALU.mult, ALU.add))
                            spacer()
                            p.op("vector", lambda e: e.tensor_tensor(
                                sa[:, C_R:C_R + 16], sa[:, C_R:C_R + 16], sa[:, C_T2:C_T2 + 16], ALU.mult))
                            spacer()
                            p.op("vector", lambda e: e.tensor_tensor(
                                sa[:, C_NB:C_NB + 16], sa[:, C_M:C_M + 16], sa[:, C_R:C_R + 16], ALU.mult))
                            spacer()
                            if dbg is not None and ei == 0:
                                p.op("vector", lambda e: e.tensor_copy(dbgT[:, 48:64], sa[:, C_R:C_R + 16]))
                            vv = p.op("vector", lambda e: e.tensor_scalar(
                                sa[:, C_NB:C_NB + 16], sa[:, C_NB:C_NB + 16], -1.0, None, ALU.mult), "V")
                            return vv

                        vstat = stats_batch()
                        if ei == 0:
                            dbg_dump("sa0", sa[:], [128, 128], F32, "V", vstat)
                        p.wt("scalar", "V", vstat)
                        s_ap = {}
                        v_ev = {}
                        t_tr = {}
                        for t in range(NT):
                            xbuf_acquire(t % 2)
                            v = dma("sync", xbuf[t % 2][:], x_in[ei, t * 128:(t + 1) * 128, :])
                            p.wt("scalar", "dS", v)
                            if t >= 2:
                                p.wt("scalar", "T", t_tr.get(t - 2))
                            s_ap[t] = p.op("scalar", lambda e, t=t: e.activation(
                                xn_tm[t % 2][:], xbuf[t % 2][:], AF.Identity,
                                bias=sa[:, C_NB + t:C_NB + t + 1],
                                scale=sa[:, C_R + t:C_R + t + 1]), "SC")
                            xbuf_g[t % 2] = [("SC", s_ap[t])]
                            p.wt("tensor", "SC", s_ap[t])
                            p.wt("tensor", "dS", init_dS)
                            if t >= 2:
                                p.wt("tensor", "V", v_ev.get(t - 2))
                            for g in range(2):
                                for i in range(4):
                                    d = g * 4 + i
                                    last = (g == 1 and i == 3)
                                    t_tr[t] = p.op("tensor", lambda e, t=t, g=g, i=i, d=d: e.transpose(
                                        ps_tr[t % 2][:, g * 512 + i * 128: g * 512 + (i + 1) * 128],
                                        xn_tm[t % 2][:, d * 128:(d + 1) * 128], id16[:]),
                                        "T" if last else None)
                            p.wt("vector", "T", t_tr[t])
                            for d in range(NDC):
                                v_ev[t] = p.op("vector", lambda e, t=t, d=d: e.tensor_copy(
                                    xn_fm[:, d * 2048 + t * 128: d * 2048 + (t + 1) * 128],
                                    ps_tr[t % 2][:, d * 128:(d + 1) * 128]),
                                    "V" if d == NDC - 1 else None)
                        p.bar()
                        dbg_dump(f"xn{ei}", xn_fm[:], [128, NDC * 2048], F16, "V", v_ev[NT - 1])
                        p.flush()

                    with contextlib.ExitStack() as es_p:
                        qk_ps = [psum(es_p, f"qkps{i}_{ei}", [128, 512], F32) for i in range(3)]
                        tbl_guard = {}
                        for (wh, bt, dst) in ((wqT_h, bq_t, q16), (wkT_h, bk_t, k16)):
                            vw = None
                            for d in range(NDC):
                                xbuf_acquire(d % 2)
                                v = dma("sync", xbuf[d % 2][:], wh[d * 128:(d + 1) * 128, :])
                                p.wt("vector", "dS", v)
                                vw = p.op("vector", lambda e, d=d: e.tensor_copy(
                                    wT16[:, d * 1024:(d + 1) * 1024], xbuf[d % 2][:]), "V")
                                xbuf_g[d % 2] = [("V", vw)]
                            p.wt("tensor", "V", vw)
                            idx = 0
                            s_ev = {}
                            for ec in range(NDC):
                                for s4 in range(4):
                                    ps = qk_ps[idx % 3]
                                    if idx >= 3:
                                        p.wt("tensor", "SC", s_ev.get(idx - 3))
                                    tv = None
                                    for d in range(NDC):
                                        tv = p.op("tensor", lambda e, ps=ps, d=d, ec=ec, s4=s4: e.matmul(
                                            ps[:], wT16[:, d * 1024 + ec * 128: d * 1024 + (ec + 1) * 128],
                                            xn_fm[:, d * 2048 + s4 * 512: d * 2048 + (s4 + 1) * 512],
                                            start=(d == 0), stop=(d == NDC - 1)),
                                            "T" if d == NDC - 1 else None)
                                    p.wt("scalar", "T", tv)
                                    s_ev[idx] = p.op("scalar", lambda e, ps=ps, ec=ec, s4=s4, bt=bt, dst=dst: e.activation(
                                        dst[:, ec * 2048 + s4 * 512: ec * 2048 + (s4 + 1) * 512],
                                        ps[:], AF.Identity, bias=bt[:, ec:ec + 1], scale=1.0), "SC")
                                    idx += 1
                            v_rope = None
                            for jc in range(4):
                                g = tbl_guard.get(jc % 2)
                                if g:
                                    p.wt("sync", "V", g)
                                vc = dma("sync", costb[jc % 2][:], cos_h[jc])
                                vs = dma("sync", sintb[jc % 2][:], sin_h[jc])
                                need = max(s_ev[(jc + 1) * 4 - 1], s_ev[(jc + 4 + 1) * 4 - 1])
                                p.wt("vector", "SC", need)
                                p.wt("vector", "dS", vs)
                                lo = dst[:, jc * 2048:(jc + 1) * 2048]
                                hi = dst[:, (jc + 4) * 2048:(jc + 5) * 2048]
                                cosj, sinj = costb[jc % 2], sintb[jc % 2]
                                p.op("vector", lambda e, lo=lo, c=cosj: e.tensor_tensor(
                                    rsc[0][:], lo, c[:], ALU.mult))
                                p.op("vector", lambda e, hi=hi, s=sinj: e.tensor_tensor(
                                    rsc[1][:], hi, s[:], ALU.mult))
                                p.op("vector", lambda e, lo=lo, s=sinj: e.tensor_tensor(
                                    rsc[2][:], lo, s[:], ALU.mult))
                                p.op("vector", lambda e, lo=lo: e.tensor_tensor(
                                    lo, rsc[0][:], rsc[1][:], ALU.subtract))
                                p.op("vector", lambda e, hi=hi, c=cosj: e.tensor_tensor(
                                    rsc[0][:], hi, c[:], ALU.mult))
                                spacer()
                                v_rope = p.op("vector", lambda e, hi=hi: e.tensor_tensor(
                                    hi, rsc[2][:], rsc[0][:], ALU.add), "V")
                                tbl_guard[jc % 2] = v_rope
                            p.bar()
                            dbg_dump(f"{'q' if dst is q16 else 'k'}{ei}", dst[:],
                                     [128, NDC * 2048], F16, "V", v_rope)
                            p.flush()

                    with contextlib.ExitStack() as es_p:
                        v_ps = [psum(es_p, f"vps{i}_{ei}", [128, 1024], F32) for i in range(2)]
                        vw = None
                        for d in range(NDC):
                            xbuf_acquire(d % 2)
                            v = dma("sync", xbuf[d % 2][:], wvT_h[d * 128:(d + 1) * 128, :])
                            p.wt("vector", "dS", v)
                            vw = p.op("vector", lambda e, d=d: e.tensor_copy(
                                wT16[:, d * 1024:(d + 1) * 1024], xbuf[d % 2][:]), "V")
                            xbuf_g[d % 2] = [("V", vw)]
                        p.wt("tensor", "V", vw)
                        s_ev = {}
                        for st in range(NT):
                            ps = v_ps[st % 2]
                            if st >= 2:
                                p.wt("tensor", "SC", s_ev.get(st - 2))
                            tv = None
                            for half in range(2):
                                p.op("tensor", lambda e, ps=ps, half=half: e.matmul(
                                    ps[:, half * 512:(half + 1) * 512], onesr[:],
                                    bv_t[0:1, half * 512:(half + 1) * 512],
                                    start=True, stop=False))
                                for d in range(NDC):
                                    tv = p.op("tensor", lambda e, ps=ps, half=half, d=d, st=st: e.matmul(
                                        ps[:, half * 512:(half + 1) * 512],
                                        xn_fm[:, d * 2048 + st * 128: d * 2048 + (st + 1) * 128],
                                        wT16[:, d * 1024 + half * 512: d * 1024 + (half + 1) * 512],
                                        start=False, stop=(d == NDC - 1)),
                                        "T" if (d == NDC - 1 and half == 1) else None)
                            p.wt("scalar", "T", tv)
                            s_ev[st] = p.op("scalar", lambda e, ps=ps, st=st: e.activation(
                                v_bf[:, st * 1024:(st + 1) * 1024], ps[:], AF.Copy), "SC")
                        p.bar()
                        dbg_dump(f"v{ei}", v_bf[:], [128, NT * 1024], BF16, "SC", s_ev[NT - 1])
                        p.flush()

                if ("B", ei) in SKIP_PHASES:
                    continue
                x1 = sb(es_e, f"x1_{ei}", [128, NT * 1024], F32, side="right")
                with contextlib.ExitStack() as es_b:
                    expT = sb(es_b, f"expT_{ei}", [128, NT * 512], BF16, side="right")
                    rs = sb(es_b, f"rs_{ei}", [128, 4], F32, side="right")
                    sc_ps = [psum(es_b, f"scps{i}_{ei}", [128, 512], F32) for i in range(3)]
                    se_ps = psum(es_b, f"seps_{ei}", [128, 512], F32)
                    pv_ps = [psum(es_b, f"pvps{i}_{ei}", [128, 1024], F32) for i in range(2)]
                    v_stt = {}
                    last_pv_T = 0
                    for qt in range(NQT):
                        s_exp = {}
                        for ks in range(NT):
                            ps = sc_ps[ks % 3]
                            if ks >= 3:
                                p.wt("tensor", "SC", s_exp.get(ks - 3))
                            if ks < 3 and last_pv_T:
                                pass
                            if qt > 0 and ks < 3:
                                p.wt("tensor", "SC", prev_exp_last)
                            tv = None
                            for ec in range(NDC):
                                tv = p.op("tensor", lambda e, ps=ps, ec=ec, ks=ks, qt=qt: e.matmul(
                                    ps[:], k16[:, ec * 2048 + ks * 128: ec * 2048 + (ks + 1) * 128],
                                    q16[:, ec * 2048 + qt * 512: ec * 2048 + (qt + 1) * 512],
                                    start=(ec == 0), stop=(ec == NDC - 1)),
                                    "T" if ec == NDC - 1 else None)
                            p.wt("scalar", "T", tv)
                            if qt > 0:
                                p.wt("scalar", "T", last_pv_T)
                            s_exp[ks] = p.op("scalar", lambda e, ps=ps, ks=ks: e.activation(
                                expT[:, ks * 512:(ks + 1) * 512], ps[:], AF.Exp,
                                bias=cneg[:, 0:1], scale=1.0 / 32.0), "SC")
                        prev_exp_last = s_exp[NT - 1]
                        p.wt("tensor", "SC", s_exp[NT - 1])
                        if qt > 0:
                            p.wt("tensor", "V", v_recips)
                        se_T = None
                        for sub in range(4):
                            for ks in range(NT):
                                se_T = p.op("tensor", lambda e, sub=sub, ks=ks: e.matmul(
                                    se_ps[:, sub:sub + 1],
                                    expT[:, ks * 512 + sub * 128: ks * 512 + (sub + 1) * 128],
                                    onesc_bf[:], start=(ks == 0), stop=(ks == NT - 1)),
                                    "T" if ks == NT - 1 else None)
                        p.wt("vector", "T", se_T)
                        v_recips = None
                        for sub in range(4):
                            v_recips = p.op("vector", lambda e, sub=sub: e.reciprocal(
                                rs[:, sub:sub + 1], se_ps[:, sub:sub + 1]), "V")
                        spacer()
                        for sub in range(4):
                            tglob = qt * 4 + sub
                            ps = pv_ps[sub % 2]
                            if tglob >= 2:
                                p.wt("tensor", "V", v_stt.get(tglob - 2))
                            xbuf_acquire(sub % 2)
                            v = dma("sync", xbuf[sub % 2][:], x_in[ei, tglob * 128:(tglob + 1) * 128, :])
                            tv = None
                            for ks in range(NT):
                                for half in range(2):
                                    tv = p.op("tensor", lambda e, ps=ps, ks=ks, half=half, sub=sub: e.matmul(
                                        ps[:, half * 512:(half + 1) * 512],
                                        expT[:, ks * 512 + sub * 128: ks * 512 + (sub + 1) * 128],
                                        v_bf[:, ks * 1024 + half * 512: ks * 1024 + (half + 1) * 512],
                                        start=(ks == 0), stop=(ks == NT - 1)),
                                        "T" if (ks == NT - 1 and half == 1) else None)
                            last_pv_T = tv
                            p.wt("vector", "T", tv)
                            p.wt("vector", "dS", v)
                            p.wt("vector", "V", v_recips)
                            v_stt[tglob] = p.op("vector", lambda e, ps=ps, sub=sub, tglob=tglob: e.scalar_tensor_tensor(
                                x1[:, tglob * 1024:(tglob + 1) * 1024], ps[:],
                                rs[:, sub:sub + 1], xbuf[sub % 2][:], ALU.mult, ALU.add), "V")
                            xbuf_g[sub % 2] = [("V", v_stt[tglob])]
                    p.bar()
                    dbg_dump(f"x1_{ei}", x1[:], [128, NT * 1024], F32, "V", v_stt[NT - 1])
                    p.flush()

                p.flush()
                es_ab.close()
                if ("C", ei) in SKIP_PHASES:
                    continue
                with contextlib.ExitStack() as es_c:
                    xq_fm = sb(es_c, f"xqfm_{ei}", [128, NDC * 2048], F16, side="left")
                    union = [sb(es_c, f"un_{ei}", [128, NCC * 512], F16, side="left")] * 2
                    w1b = [sb(es_c, f"w1b{i}_{ei}", [128, NDC * 128], F16, side="left") for i in range(2)]
                    w2b = [sb(es_c, f"w2b{i}_{ei}", [128, 512], F16, side="left") for i in range(2)]
                    hs = [sb(es_c, f"hs{i}_{ei}", [128, 514], F16, side="left") for i in range(3)]
                    cs1 = [sb(es_c, f"cs1{i}_{ei}", [128, 512], F16, side="left") for i in range(2)]
                    cs2 = [sb(es_c, f"cs2{i}_{ei}", [128, 512], F16, side="left") for i in range(2)]
                    uq32 = [sb(es_c, f"uq32{i}_{ei}", [128, 512], F32, side="left") for i in range(2)]
                    u32b = [sb(es_c, f"u32b{i}_{ei}", [128, D], F32, side="left") for i in range(2)]
                    xq_tm = [sb(es_c, f"xqtm{i}_{ei}", [128, D], F16, side="left") for i in range(2)]
                    x2st = [sb(es_c, f"x2st{i}_{ei}", [128, D], F32, side="left") for i in range(4)]
                    shv = sb(es_c, f"shv_{ei}", [128, 4], F32, side="left")
                    comb2 = sb(es_c, f"comb2_{ei}", [128, 4], F32, side="left")
                    sh16 = sb(es_c, f"sh16_{ei}", [128, 4], F16, side="left")

                    with contextlib.ExitStack() as es_p:
                        ps_tr = [psum(es_p, f"cpstr{i}_{ei}", [128, 1024], F16) for i in range(2)]
                        s_sq = {}
                        ssq_red = {}
                        for t in range(NT):
                            p.op("vector", lambda e, t=t: e.reduce_sum(
                                sa[:, C_SUM + t:C_SUM + t + 1],
                                x1[:, t * 1024:(t + 1) * 1024], axis=AX.X))
                            if t >= 2:
                                p.wt("scalar", "V", ssq_red.get(t - 2))
                            s_sq[t] = p.op("scalar", lambda e, t=t: e.activation(
                                scr1k[t % 2][:], x1[:, t * 1024:(t + 1) * 1024], AF.Square), "SC")
                            p.wt("vector", "SC", s_sq[t])
                            ssq_red[t] = p.op("vector", lambda e, t=t: e.reduce_sum(
                                sa[:, C_SSQ + t:C_SSQ + t + 1], scr1k[t % 2][:], axis=AX.X), "V")
                        vstat = stats_batch()
                        v_q = {}
                        v_ev2 = {}
                        t_tr = {}
                        for t in range(NT):
                            p.op("vector", lambda e, t=t: e.tensor_scalar(
                                u32b[t % 2][:], x1[:, t * 1024:(t + 1) * 1024],
                                sa[:, C_M + t:C_M + t + 1], sa[:, C_R + t:C_R + t + 1],
                                ALU.subtract, ALU.mult))
                            spacer()
                            p.op("vector", lambda e, t=t: e.reduce_max(
                                sa[:, C_T1:C_T1 + 1], u32b[t % 2][:], axis=AX.X,
                                apply_absolute_value=True))
                            spacer()
                            p.op("vector", lambda e: e.tensor_scalar(
                                sa[:, C_T1:C_T1 + 1], sa[:, C_T1:C_T1 + 1], 1e-5, None, ALU.max))
                            spacer()
                            if dbg is not None and ei == 0:
                                p.op("vector", lambda e, t=t: e.tensor_copy(dbgT[:, t:t + 1], sa[:, C_T1:C_T1 + 1]))
                            p.op("vector", lambda e: e.reciprocal(
                                sa[:, C_T2:C_T2 + 1], sa[:, C_T1:C_T1 + 1]))
                            spacer()
                            p.op("vector", lambda e: e.tensor_scalar(
                                sa[:, C_T2:C_T2 + 1], sa[:, C_T2:C_T2 + 1], 127.0, None, ALU.mult))
                            p.op("vector", lambda e: e.tensor_tensor(
                                sa[:, C_T1:C_T1 + 1], sa[:, C_T1:C_T1 + 1], mw1c[:], ALU.mult))
                            spacer()
                            spacer()
                            p.op("vector", lambda e, t=t: e.tensor_scalar(
                                u32b[t % 2][:], u32b[t % 2][:], sa[:, C_T2:C_T2 + 1], MAGIC,
                                ALU.mult, ALU.add))
                            spacer()
                            if t >= 2:
                                p.wt("vector", "T", t_tr.get(t - 2))
                            v_q[t] = p.op("vector", lambda e, t=t: e.tensor_scalar(
                                xq_tm[t % 2][:], u32b[t % 2][:], MAGIC, sa[:, C_T1:C_T1 + 1],
                                ALU.subtract, ALU.mult), "V")
                            p.wt("tensor", "V", v_q[t])
                            if t >= 2:
                                p.wt("tensor", "V", v_ev2.get(t - 2))
                            for g in range(2):
                                for i in range(4):
                                    d = g * 4 + i
                                    t_tr[t] = p.op("tensor", lambda e, t=t, g=g, i=i, d=d: e.transpose(
                                        ps_tr[t % 2][:, g * 512 + i * 128: g * 512 + (i + 1) * 128],
                                        xq_tm[t % 2][:, d * 128:(d + 1) * 128], id16[:]),
                                        "T" if (g == 1 and i == 3) else None)
                            p.wt("vector", "T", t_tr[t])
                            for d in range(NDC):
                                v_ev2[t] = p.op("vector", lambda e, t=t, d=d: e.tensor_copy(
                                    xq_fm[:, d * 2048 + t * 128: d * 2048 + (t + 1) * 128],
                                    ps_tr[t % 2][:, d * 128:(d + 1) * 128]),
                                    "V" if d == NDC - 1 else None)
                        v_hal = None
                        for d in range(NDC):
                            for bi, col in enumerate((511, 1023, 1535)):
                                v_hal = p.op("vector", lambda e, d=d, bi=bi, col=col: e.tensor_copy(
                                    xqhalo[:, d * 6 + bi * 2: d * 6 + bi * 2 + 2],
                                    xq_fm[:, d * 2048 + col: d * 2048 + col + 2]), "V")
                        p.bar()
                        dbg_dump(f"xq{ei}", xq_fm[:], [128, NDC * 2048], F16, "V", v_hal)
                        p.flush()

                    with contextlib.ExitStack() as es_p:
                        h_ps = [psum(es_p, f"hps{i}_{ei}", [128, 512], F32) for i in range(2)]
                        hb_ps = psum(es_p, f"hbps_{ei}", [128, 512], F32)
                        tr_ps = psum(es_p, f"trps_{ei}", [128, 1024], F16)
                        o2_ps = [psum(es_p, f"o2ps{i}_{ei}", [128, 512], F32) for i in range(4)]
                        p.wt("vector", "dO", p.cnt["dO"])
                        p.wt("sync", "T", p.cnt["T"])
                        p.wt("sync", "dS", prep_scr_dS)

                        s_hs = {}
                        v_conv = {}
                        s_silu = {}
                        t_mm1 = {}
                        t_w1used = {}
                        t_mm2w = {}
                        g_fill = {}
                        v_quant = {}
                        t_mm2last = {0: 0, 1: 0}
                        v_o2 = {}
                        d_out = {}
                        s_halo = {}

                        for j in range(NSCH):
                            jp = j % 2
                            un = union[jp]
                            silu_guard = t_mm2last.get(jp, 0)
                            for c in range(NCC):
                                ci = j * NCC + c
                                if ci >= 2:
                                    p.wt("sync", "T", t_w1used.get(ci - 2))
                                vw = dma("sync", w1b[ci % 2][:], w1q_h[c])
                                p.wt("tensor", "dS", vw)
                                if ci >= 2:
                                    p.wt("tensor", "SC", s_hs.get(ci - 2))
                                tv = None
                                for d in range(NDC):
                                    tv = p.op("tensor", lambda e, ci=ci, d=d, j=j: e.matmul(
                                        h_ps[ci % 2][:],
                                        w1b[ci % 2][:, d * 128:(d + 1) * 128],
                                        xq_fm[:, d * 2048 + j * 512: d * 2048 + (j + 1) * 512],
                                        start=(d == 0), stop=(d == NDC - 1)),
                                        "T" if d == NDC - 1 else None)
                                t_mm1[ci] = tv
                                t_w1used[ci] = tv
                                if j == 0:
                                    if c > 0:
                                        p.wt("tensor", "SC", s_halo.get(c - 1))
                                    for d in range(NDC):
                                        tv2 = p.op("tensor", lambda e, ci=ci, d=d: e.matmul(
                                            hb_ps[:, 0:6],
                                            w1b[ci % 2][:, d * 128:(d + 1) * 128],
                                            xqhalo[:, d * 6:(d + 1) * 6],
                                            start=(d == 0), stop=(d == NDC - 1)),
                                            "T" if d == NDC - 1 else None)
                                    t_w1used[ci] = tv2
                                    p.wt("scalar", "T", tv2)
                                    s_halo[c] = p.op("scalar", lambda e, c=c: e.activation(
                                        hhalo[:, c * 6:(c + 1) * 6], hb_ps[:, 0:6], AF.Copy), "SC")
                                p.wt("scalar", "T", t_mm1[ci])
                                if ci >= 3:
                                    p.wt("scalar", "V", v_conv.get(ci - 3))
                                s_hs[ci] = p.op("scalar", lambda e, ci=ci: e.activation(
                                    hs[ci % 3][:, 1:513], h_ps[ci % 2][:], AF.Copy), "SC")
                                p.wt("gpsimd", "SC", s_hs[ci])
                                if j == 0:
                                    p.wt("gpsimd", "SC", s_halo[c])
                                if j == 0:
                                    p.op("gpsimd", lambda e, ci=ci: e.memset(hs[ci % 3][:, 0:1], 0.0))
                                else:
                                    p.op("gpsimd", lambda e, ci=ci, c=c, j=j: e.tensor_copy(
                                        hs[ci % 3][:, 0:1], hhalo[:, c * 6 + 2 * (j - 1): c * 6 + 2 * (j - 1) + 1]))
                                if j == NSCH - 1:
                                    g_fill[ci] = p.op("gpsimd", lambda e, ci=ci: e.memset(
                                        hs[ci % 3][:, 513:514], 0.0), "G")
                                else:
                                    g_fill[ci] = p.op("gpsimd", lambda e, ci=ci, c=c, j=j: e.tensor_copy(
                                        hs[ci % 3][:, 513:514], hhalo[:, c * 6 + 2 * j + 1: c * 6 + 2 * j + 2]), "G")
                                p.wt("vector", "SC", s_hs[ci])
                                p.wt("vector", "G", g_fill[ci])
                                if ci >= 2:
                                    p.wt("vector", "SC", s_silu.get(ci - 2))
                                p.op("vector", lambda e, ci=ci, c=c: e.tensor_scalar(
                                    cs1[ci % 2][:], hs[ci % 3][:, 1:513],
                                    cw_t[:, 32 + c:33 + c], cb_t[:, c:c + 1], ALU.mult, ALU.add))
                                p.op("vector", lambda e, ci=ci, c=c: e.scalar_tensor_tensor(
                                    cs2[ci % 2][:], hs[ci % 3][:, 0:512],
                                    cw_t[:, c:c + 1], cs1[ci % 2][:], ALU.mult, ALU.add))
                                v_conv[ci] = p.op("vector", lambda e, ci=ci, c=c: e.scalar_tensor_tensor(
                                    cs1[ci % 2][:], hs[ci % 3][:, 2:514],
                                    cw_t[:, 64 + c:65 + c], cs2[ci % 2][:], ALU.mult, ALU.add), "V")
                                p.wt("scalar", "V", v_conv[ci])
                                p.wt("scalar", "T", silu_guard)
                                s_silu[ci] = p.op("scalar", lambda e, ci=ci, c=c, un=un: e.activation(
                                    un[:, c * 512:(c + 1) * 512], cs1[ci % 2][:], AF.Silu), "SC")
                                p.wt("vector", "SC", s_silu[ci])
                                if c == 0:
                                    p.op("vector", lambda e, un=un: e.tensor_copy(mmax[:], un[:, 0:512]))
                                    p.op("vector", lambda e, un=un: e.tensor_copy(mmin[:], un[:, 0:512]))
                                else:
                                    p.op("vector", lambda e, un=un, c=c: e.tensor_tensor(
                                        mmax[:], mmax[:], un[:, c * 512:(c + 1) * 512], ALU.max))
                                    p.op("vector", lambda e, un=un, c=c: e.tensor_tensor(
                                        mmin[:], mmin[:], un[:, c * 512:(c + 1) * 512], ALU.min))

                            p.op("vector", lambda e: e.tensor_scalar(
                                mmin[:], mmin[:], -1.0, None, ALU.mult))
                            spacer()
                            v_mabs = p.op("vector", lambda e: e.tensor_tensor(
                                mmax[:], mmax[:], mmin[:], ALU.max), "V")
                            p.wt("tensor", "V", v_mabs)
                            t_trm = None
                            for st in range(4):
                                t_trm = p.op("tensor", lambda e, st=st: e.transpose(
                                    tr_ps[:, st * 128:(st + 1) * 128],
                                    mmax[:, st * 128:(st + 1) * 128], id16[:]),
                                    "T" if st == 3 else None)
                            p.wt("vector", "T", t_trm)
                            for st in range(4):
                                p.op("vector", lambda e, st=st: e.reduce_max(
                                    shv[:, st:st + 1], tr_ps[:, st * 128:(st + 1) * 128],
                                    axis=AX.X))
                            spacer()
                            p.op("vector", lambda e: e.tensor_scalar(
                                shv[:], shv[:], 1e-5, None, ALU.max))
                            spacer()
                            p.op("vector", lambda e: e.tensor_scalar(
                                comb2[:], shv[:], mw2c[:, 0:1], None, ALU.mult))
                            p.op("vector", lambda e: e.reciprocal(shv[:], shv[:]))
                            spacer()
                            p.op("vector", lambda e: e.tensor_scalar(
                                shv[:], shv[:], 127.0, None, ALU.mult))
                            spacer()
                            v_sh = p.op("vector", lambda e: e.tensor_copy(sh16[:], shv[:]), "V")
                            p.wt("tensor", "V", v_sh)
                            t_trs = None
                            for st in range(4):
                                t_trs = p.op("tensor", lambda e, st=st: e.transpose(
                                    tr_ps[0:1, 512 + st * 128: 512 + (st + 1) * 128],
                                    sh16[:, st:st + 1], id16[:]), "T" if st == 3 else None)
                            p.wt("scalar", "T", t_trs)
                            s_srow = p.op("scalar", lambda e: e.activation(
                                srow[:], tr_ps[0:1, 512:1024], AF.Copy), "SC")
                            p.wt("tensor", "SC", s_srow)
                            p.wt("tensor", "G", g_init)
                            t_bh = p.op("tensor", lambda e: e.matmul(
                                hb_ps[:], onesr16[:], srow[:], start=True, stop=True), "T")
                            p.wt("vector", "T", t_bh)
                            v_bh = p.op("vector", lambda e: e.tensor_copy(bh16[:], hb_ps[:]), "V")

                            for c in range(NCC):
                                ci = j * NCC + c
                                p.op("vector", lambda e, un=un, c=c, ci=ci: e.tensor_tensor(
                                    uq32[ci % 2][:], un[:, c * 512:(c + 1) * 512], bh16[:], ALU.mult))
                                v_quant[ci] = p.op("vector", lambda e, un=un, c=c, ci=ci: e.tensor_scalar(
                                    un[:, c * 512:(c + 1) * 512], uq32[ci % 2][:], MAGIC, MAGIC,
                                    ALU.add, ALU.subtract), "V")

                            for dh in range(2):
                                for c in range(NCC):
                                    wi = (ei * NSCH * 2 + j * 2 + dh) * NCC + c
                                    if wi >= 2:
                                        p.wt("sync", "T", t_mm2w.get(wi - 2))
                                    vw = dma("sync", w2b[wi % 2][:],
                                             w2q_h[c][:, dh * 512:(dh + 1) * 512])
                                    p.wt("tensor", "dS", vw)
                                    p.wt("tensor", "V", v_quant[j * NCC + c])
                                    if c == 0:
                                        for st in range(4):
                                            p.wt("tensor", "V", v_o2.get((j * 2 + dh - 1, st), 0))
                                    tv = None
                                    for st in range(4):
                                        tv = p.op("tensor", lambda e, un=un, c=c, st=st, dh=dh, wi=wi: e.matmul(
                                            o2_ps[st][:],
                                            un[:, c * 512 + st * 128: c * 512 + (st + 1) * 128],
                                            w2b[wi % 2][:],
                                            start=(c == 0), stop=(c == NCC - 1)),
                                            "T" if (st == 3) else None)
                                    t_mm2w[wi] = tv
                                t_mm2last[jp] = tv
                                for st in range(4):
                                    tglob = j * 4 + st
                                    p.wt("vector", "T", tv)
                                    if tglob >= 4 and dh == 0:
                                        p.wt("vector", "dO", d_out.get(tglob - 4))
                                    v_o2[(j * 2 + dh, st)] = p.op(
                                        "vector", lambda e, st=st, dh=dh, tglob=tglob: e.scalar_tensor_tensor(
                                            x2st[st][:, dh * 512:(dh + 1) * 512],
                                            o2_ps[st][:], comb2[:, st:st + 1],
                                            x1[:, tglob * 1024 + dh * 512: tglob * 1024 + (dh + 1) * 512],
                                            ALU.mult, ALU.add), "V")
                                    if dh == 1:
                                        p.wt("scalar", "V", v_o2[(j * 2 + 1, st)])
                                        p.wt("scalar", "V", v_o2[(j * 2, st)])
                                        d_out[tglob] = dma(
                                            "scalar", out_h[ei, tglob * 128:(tglob + 1) * 128, :],
                                            x2st[st][:])
                            p.bar()
                            p.flush()

        if dbg is not None:
            p.bar()
            dbg_dump("dbgT", dbgT[:], [128, 64], F32, "bar", p.cnt["bar"])
        p.wt("sync", "dO", p.cnt["dO"])
        p.wt("sync", "dS", p.cnt["dS"])
        p.flush()
        p.check_deadlock()

    if dbg is not None:
        dbg.update(dbg_outs)
    return nc


def make_inputs(L0_attn_x, L0_attn_wq_w, L0_attn_wq_b, L0_attn_wk_w, L0_attn_wk_b,
                L0_attn_wv_w, L0_attn_wv_b, L0_ffn_w1, L0_ffn_conv_w, L0_ffn_conv_b,
                L0_ffn_w2):
    perm = np.concatenate([np.arange(0, D, 2), np.arange(1, D, 2)])
    wq = np.asarray(L0_attn_wq_w, dtype=np.float32)
    wk = np.asarray(L0_attn_wk_w, dtype=np.float32)
    wv = np.asarray(L0_attn_wv_w, dtype=np.float32)
    wqT = np.ascontiguousarray(wq[perm].T)
    wkT = np.ascontiguousarray(wk[perm].T)
    wvT = np.ascontiguousarray(wv.T)
    bq = np.ascontiguousarray(np.asarray(L0_attn_wq_b, np.float32)[perm].reshape(8, 128).T)
    bk = np.ascontiguousarray(np.asarray(L0_attn_wk_b, np.float32)[perm].reshape(8, 128).T)
    bv = np.ascontiguousarray(np.asarray(L0_attn_wv_b, np.float32).reshape(1, D))
    w1T = np.ascontiguousarray(np.asarray(L0_ffn_w1, np.float32).T)
    w2T = np.ascontiguousarray(np.asarray(L0_ffn_w2, np.float32).T)
    cwn = np.asarray(L0_ffn_conv_w, np.float32)[:, 0, :]
    cw = np.ascontiguousarray(
        cwn.T.reshape(3, NCC, 128).transpose(2, 0, 1).reshape(128, 3 * NCC))
    cb = np.ascontiguousarray(np.asarray(L0_ffn_conv_b, np.float32).reshape(NCC, 128).T)
    j = np.arange(D // 2, dtype=np.float64)
    inv = 10000.0 ** (-j / (D // 2))
    ang = np.arange(S, dtype=np.float64)[None, :] * inv[:, None]
    cost = np.cos(ang).astype(np.float16).reshape(4, 128, S)
    sint = np.sin(ang).astype(np.float16).reshape(4, 128, S)
    id16 = np.eye(128, dtype=np.float16)
    onesr = np.ones((1, 128), dtype=np.float32)

    x = np.asarray(L0_attn_x, np.float32).reshape(NCORES, EPB, S, D)
    common = dict(wqT=wqT, wkT=wkT, wvT=wvT, bq=bq, bk=bk, bv=bv, w1T=w1T, w2T=w2T,
                  cw=cw, cb=cb, cost=cost, sint=sint, id16=id16, onesr=onesr)
    return [dict(x=np.ascontiguousarray(x[i]), **common) for i in range(NCORES)]


def kernel(**inputs):
    in_maps = make_inputs(**inputs)
    nc = build_program()
    res = run_bass_kernel_spmd(nc, in_maps, core_ids=list(range(NCORES)))
    out = np.concatenate([np.asarray(r["out"]).reshape(EPB, S, D) for r in res.results], 0)
    return np.ascontiguousarray(out.reshape(B, S, D)).astype(np.float32)
